# version 02
# baseline (speedup 1.0000x reference)
import numpy as np
import concourse.bass as bass
import concourse.mybir as mybir
from concourse.bass_utils import run_bass_kernel_spmd

F32 = mybir.dt.float32
BF16 = mybir.dt.bfloat16
AF = mybir.ActivationFunctionType
OP = mybir.AluOpType
AX = mybir.AxisListType

NCORES = 8
T = 2048
DM = 1024
DS = 64
LL = 64
FF = 4096
NE = 8
NSUP = 2
TSUP = T // NSUP
NC_DM = DM // 128
NFC = FF // 128
NG = 4
GF = 8
RA = 5
RB = 6
PADL, PADR = 32, 31
CW = PADL + T + PADR


class _Sched:
    def __init__(self):
        self.ops = {k: [] for k in ("sync", "gpsimd", "scalar", "vector", "tensor")}
        self.nv = 0
        self.ns = 0
        self.nt = 0
        self.ndx = 0
        self.ndw = 0
        self.lw = {k: {} for k in self.ops}

    def emit(self, eng, fn):
        self.ops[eng].append(fn)

    def wait(self, eng, key, val):
        if val is None or val <= 0:
            return
        if self.lw[eng].get(key, 0) >= val:
            return
        self.lw[eng][key] = val
        self.emit(eng, lambda E, sems, k=key, v=val: E.wait_ge(sems[k], v))


def build_nc():
    nc = bass.Bass("TRN2", target_bir_lowering=False, detect_race_conditions=False)

    u_t = nc.dram_tensor("u", [T, DM], F32, kind="ExternalInput")
    w1_t = nc.dram_tensor("w1", [NE, DM, FF], F32, kind="ExternalInput")
    w3_t = nc.dram_tensor("w3", [NE, DM, FF], F32, kind="ExternalInput")
    w2_t = nc.dram_tensor("w2", [NE, FF, DM], F32, kind="ExternalInput")
    wg_t = nc.dram_tensor("wg2", [128, 64], F32, kind="ExternalInput")
    a_t = nc.dram_tensor("a_v", [1, DS], F32, kind="ExternalInput")
    b_t = nc.dram_tensor("b_m", [DS, DM], F32, kind="ExternalInput")
    c_t = nc.dram_tensor("c_m", [DM, DS], F32, kind="ExternalInput")
    d_t = nc.dram_tensor("d2", [128, NC_DM], F32, kind="ExternalInput")
    ldt_t = nc.dram_tensor("logdt2", [128, NC_DM], F32, kind="ExternalInput")
    id_t = nc.dram_tensor("ident", [128, 128], F32, kind="ExternalInput")
    ramp_t = nc.dram_tensor("ramp", [128, LL * DS], F32, kind="ExternalInput")
    out_t = nc.dram_tensor("out", [T, DM], F32, kind="ExternalOutput")

    from contextlib import ExitStack
    with ExitStack() as ctx:
        EC = ctx.enter_context
        BIG = EC(nc.sbuf_tensor("BIG", [128, 16384], F32))
        xnf = EC(nc.sbuf_tensor("xnf", [128, NC_DM * CW], BF16))
        w1r = EC(nc.sbuf_tensor("w1r", [128, RA * 2048], BF16))
        w3r = EC(nc.sbuf_tensor("w3r", [128, RA * 2048], BF16))
        w2r = EC(nc.sbuf_tensor("w2r", [128, RB * 2048], BF16))
        g_sb = EC(nc.sbuf_tensor("g_sb", [128, GF * TSUP], BF16))
        s_sb = EC(nc.sbuf_tensor("s_sb", [128, 2 * TSUP], BF16))
        convb = EC(nc.sbuf_tensor("convb", [128, TSUP], F32))
        kt = EC(nc.sbuf_tensor("kt", [128, NC_DM * LL], F32))
        bct = EC(nc.sbuf_tensor("bct", [128, NC_DM * DS], F32))
        pt_sb = EC(nc.sbuf_tensor("pt_sb", [128, NC_DM * DS], F32))
        a_b = EC(nc.sbuf_tensor("a_b", [128, DS], F32))
        dtc = EC(nc.sbuf_tensor("dtc", [128, NC_DM], F32))
        d_sb = EC(nc.sbuf_tensor("d_sb", [128, NC_DM], F32))
        ldt_sb = EC(nc.sbuf_tensor("ldt_sb", [128, NC_DM], F32))
        wg_sb = EC(nc.sbuf_tensor("wg_sb", [128, 64], F32))
        ident = EC(nc.sbuf_tensor("identt", [128, 128], F32))
        diag = EC(nc.sbuf_tensor("diag", [128, 2 * 128], BF16))
        tmp32 = EC(nc.sbuf_tensor("tmp32", [128, 8 * 128], F32))
        b_sb = EC(nc.sbuf_tensor("b_sb", [DS, DM], F32))
        c_sb = EC(nc.sbuf_tensor("c_sb", [128, NC_DM * DS], F32))
        csr = EC(nc.sbuf_tensor("csr", [1, DS], F32))
        csc = EC(nc.sbuf_tensor("csc", [DS, 1], F32))
        ones_c = EC(nc.sbuf_tensor("ones_c", [128, 1], F32))
        ones_r = EC(nc.sbuf_tensor("ones_r", [1, 128], F32))
        stats = EC(nc.sbuf_tensor("stats", [128, 12], F32))
        aggr16 = EC(nc.sbuf_tensor("aggr16", [128, 32], F32))
        rstd16 = EC(nc.sbuf_tensor("rstd16", [128, 16], F32))
        sq16 = EC(nc.sbuf_tensor("sq16", [128, 16], F32))
        lgb = EC(nc.sbuf_tensor("lgb", [128, 8], F32))
        eps_sb = EC(nc.sbuf_tensor("eps_sb", [128, 1], F32))
        rsc = EC(nc.sbuf_tensor("rsc", [128, 48], F32))
        wtok = EC(nc.sbuf_tensor("wtok", [128, 128], F32))
        H1 = EC(nc.psum_tensor("H1", [128, 1024], F32))
        H3 = EC(nc.psum_tensor("H3", [128, 1024], F32))
        YA = EC(nc.psum_tensor("YA", [128, 1024], F32))
        YB = EC(nc.psum_tensor("YB", [128, 1024], F32))
        sem_handles = {}
        for k in ("T", "V", "S", "DW", "GF", "U", "P", "R", "I0", "I1", "O"):
            sem_handles[k] = EC(nc.semaphore("sem" + k))

        S = _Sched()

        def vop(fn, waits=()):
            for k, v in waits:
                S.wait("vector", k, v)
            S.wait("vector", "V", S.nv)
            S.emit("vector", lambda E, sems, f=fn: f(E).then_inc(sems["V"], 1))
            S.nv += 1
            return S.nv

        def sop(fn, waits=()):
            for k, v in waits:
                S.wait("scalar", k, v)
            S.wait("scalar", "S", S.ns)
            S.emit("scalar", lambda E, sems, f=fn: f(E).then_inc(sems["S"], 1))
            S.ns += 1
            return S.ns

        def top(fn, waits=(), inc=False):
            for k, v in waits:
                S.wait("tensor", k, v)
            if inc:
                S.emit("tensor", lambda E, sems, f=fn: f(E).then_inc(sems["T"], 1))
                S.nt += 1
                return S.nt
            S.emit("tensor", lambda E, sems, f=fn: f(E))
            return None

        def dx(fn, sem, waits=()):
            for k, v in waits:
                S.wait("sync", k, v)
            S.emit("sync", lambda E, sems, f=fn, sm=sem: f(E).then_inc(sems[sm], 16))
            S.ndx += 16
            return S.ndx

        def dow(fn, waits=()):
            for k, v in waits:
                S.wait("gpsimd", k, v)
            S.emit("gpsimd", lambda E, sems, f=fn: f(E).then_inc(sems["DW"], 16))
            S.ndw += 16
            return S.ndw

        A = bass.AP

        v_u = 16
        dx(lambda E: E.dma_start(
            out=A(BIG, 0, [[16384, 128], [1024, 16], [1, 1024]]),
            in_=A(u_t, 0, [[1024, 128], [131072, 16], [1, 1024]])), "U")
        dx(lambda E: E.dma_start(out=ident[:, :], in_=id_t[:, :]), "P")
        dx(lambda E: E.dma_start(
            out=A(c_sb, 0, [[512, 128], [64, 8], [1, 64]]),
            in_=A(c_t, 0, [[64, 128], [8192, 8], [1, 64]])), "P")
        dx(lambda E: E.dma_start(out=b_sb[:, :], in_=b_t[:, :]), "P")
        dx(lambda E: E.dma_start(out=A(a_b, 0, [[64, 1], [1, 64]]),
                                  in_=a_t[:, :]), "P")
        dx(lambda E: E.dma_start(out=ldt_sb[:, :], in_=ldt_t[:, :]), "P")
        dx(lambda E: E.dma_start(out=d_sb[:, :], in_=d_t[:, :]), "P")
        dx(lambda E: E.dma_start(out=wg_sb[:, :], in_=wg_t[:, :]), "P")
        v_par = 7 * 16

        vop(lambda E: E.memset(ones_c[:, :], 1.0))
        vop(lambda E: E.memset(eps_sb[:, :], 1e-5))
        vop(lambda E: E.memset(ones_r[:, :], 1.0))
        vop(lambda E: E.memset(A(xnf, 0, [[NC_DM * CW, 128], [CW, NC_DM], [1, PADL]]), 0.0))
        v_const = vop(lambda E: E.memset(
            A(xnf, PADL + T, [[NC_DM * CW, 128], [CW, NC_DM], [1, PADR]]), 0.0))

        for cp in range(NC_DM):
            t_csum = top(lambda E, cp=cp: E.matmul(
                YA[0:1, cp * 64:(cp + 1) * 64], ones_c[:, 0:1],
                c_sb[:, cp * 64:(cp + 1) * 64], start=True, stop=True),
                waits=([("P", v_par), ("V", v_const)] if cp == 0 else ()),
                inc=(cp == NC_DM - 1))
        v_csr = vop(lambda E: E.tensor_reduce(
            csr[0:1, 0:DS], A(YA, 0, [[1024, 1], [1, DS], [64, NC_DM]]),
            axis=AX.X, op=OP.add), waits=[("T", t_csum)])
        t_csc = top(lambda E: E.transpose(YB[0:DS, 0:1], csr[0:1, 0:DS], ident[0:1, 0:1]),
                    waits=[("V", v_csr), ("P", v_par)], inc=True)
        v_csc = vop(lambda E: E.tensor_copy(csc[:, :], YB[0:DS, 0:1]), waits=[("T", t_csc)])
        v_bc = vop(lambda E: E.tensor_scalar(b_sb[:, :], b_sb[:, :], csc[:, 0:1], None,
                                             op0=OP.mult),
                   waits=[("P", v_par)])
        for cp in range(NC_DM):
            t_tr = top(lambda E, cp=cp: E.transpose(
                YA[:, cp * 64:(cp + 1) * 64], b_sb[:, cp * 128:(cp + 1) * 128],
                ident[0:DS, 0:DS]),
                waits=([("V", v_bc)] if cp == 0 else ()), inc=(cp == NC_DM - 1))
        t_bct = t_tr
        v_bct = vop(lambda E: E.tensor_copy(bct[:, :], YA[:, 0:512]), waits=[("T", t_bct)])
        t_ab = top(lambda E: E.matmul(YB[:, 0:DS], ones_r[0:1, 0:128],
                                      A(a_b, 0, [[64, 1], [1, 64]]), start=True, stop=True),
                   waits=[("P", v_par), ("V", v_csc)], inc=True)
        v_ab = vop(lambda E: E.tensor_copy(a_b[:, :], YB[:, 0:DS]), waits=[("T", t_ab)])
        s_dtc = sop(lambda E: E.activation(dtc[:, :], ldt_sb[:, :], AF.Exp),
                    waits=[("P", v_par)])
        for c in range(NC_DM):
            v_pt = vop(lambda E, c=c: E.tensor_scalar(
                pt_sb[:, c * DS:(c + 1) * DS], a_b[:, :], dtc[:, c:c + 1], None,
                op0=OP.mult), waits=[("S", s_dtc)])
        v_ptdone = v_pt

        for tb in range(16):
            vop(lambda E, tb=tb: E.bn_stats(stats[:, 0:6], BIG[:, tb * 1024:tb * 1024 + 512]),
                waits=[("U", 16)])
            vop(lambda E, tb=tb: E.bn_stats(stats[:, 6:12],
                                            BIG[:, tb * 1024 + 512:tb * 1024 + 1024]))
            vop(lambda E, tb=tb: E.bn_aggr(aggr16[:, 2 * tb:2 * tb + 2], stats[:, 0:12]))
        v_aggr = S.nv
        for tb in range(16):
            sop(lambda E, tb=tb: E.activation(sq16[:, tb:tb + 1],
                                              aggr16[:, 2 * tb + 1:2 * tb + 2],
                                              AF.Sqrt, bias=eps_sb[:, 0:1]),
                waits=[("V", v_aggr)])
        s_sq = S.ns
        for tb in range(16):
            vop(lambda E, tb=tb: E.reciprocal(rstd16[:, tb:tb + 1], sq16[:, tb:tb + 1]),
                waits=[("S", s_sq)])
            vop(lambda E, tb=tb: E.tensor_scalar(
                BIG[:, tb * 1024:(tb + 1) * 1024], BIG[:, tb * 1024:(tb + 1) * 1024],
                aggr16[:, 2 * tb:2 * tb + 1], rstd16[:, tb:tb + 1],
                op0=OP.subtract, op1=OP.mult))
        v_lnapply = S.nv

        ev_half = {}
        ev_clk = {}
        mm_clk = {}
        lg_clk = {}
        t_logit_last = 0
        for tb in range(16):
            t_half = {}
            for half in range(2):
                for c in range(4 * half, 4 * half + 4):
                    w = [("V", v_lnapply)]
                    if tb > 0:
                        w.append(("V", ev_half[(tb - 1, half)]))
                    elif half == 0 and c == 0:
                        w.append(("V", v_ab))
                    top(lambda E, tb=tb, c=c: E.transpose(
                        YB[:, c * 128:(c + 1) * 128],
                        BIG[:, tb * 1024 + c * 128:tb * 1024 + (c + 1) * 128],
                        ident[:, :]), waits=(w if c == 4 * half else ()),
                        inc=(c == 4 * half + 3))
                t_half[half] = S.nt
            for half in range(2):
                for c in range(4 * half, 4 * half + 4):
                    vop(lambda E, tb=tb, c=c: E.tensor_copy(
                        xnf[:, c * CW + PADL + tb * 128:c * CW + PADL + (tb + 1) * 128],
                        YB[:, c * 128:(c + 1) * 128]), waits=[("T", t_half[half])])
                    ev_clk[(tb, c)] = vop(lambda E, tb=tb, c=c: E.tensor_copy(
                        tmp32[:, c * 128:(c + 1) * 128],
                        YB[:, c * 128:(c + 1) * 128]))
                ev_half[(tb, half)] = ev_clk[(tb, 4 * half + 3)]
            for c in range(NC_DM):
                w = [("V", ev_clk[(tb, c)]), ("P", v_par)]
                if c == 0 and tb > 0:
                    w.append(("V", lg_clk[tb - 1]))
                elif c == 0:
                    w.append(("V", v_bct))
                mm_clk[(tb, c)] = top(lambda E, tb=tb, c=c: E.matmul(
                    YA[:, 0:8], tmp32[:, c * 128:(c + 1) * 128],
                    wg_sb[:, c * 8:(c + 1) * 8], start=(c == 0), stop=(c == NC_DM - 1)),
                    waits=w, inc=True)
            t_logit_last = mm_clk[(tb, NC_DM - 1)]
            m1 = rsc[:, 0:1]; m2 = rsc[:, 1:2]; dd = rsc[:, 2:3]
            w1s = rsc[:, 3:4]; w2s = rsc[:, 4:5]
            l2 = rsc[:, 8:16]; eq1 = rsc[:, 16:24]; eq2 = rsc[:, 24:32]
            lg_clk[tb] = vop(lambda E: E.tensor_copy(lgb[:, :], YA[:, 0:8]),
                             waits=[("T", t_logit_last)])
            vop(lambda E: E.reduce_max(m1, lgb[:, :], axis=AX.X))
            vop(lambda E: E.tensor_scalar(eq1, lgb[:, :], m1, None, op0=OP.is_equal))
            v_lgfree = vop(lambda E: E.scalar_tensor_tensor(l2, eq1, -1e30, lgb[:, :],
                                                            op0=OP.mult, op1=OP.add))
            vop(lambda E: E.reduce_max(m2, l2, axis=AX.X))
            vop(lambda E: E.tensor_scalar(eq2, l2, m2, None, op0=OP.is_equal))
            v_d2 = vop(lambda E: E.tensor_tensor(dd, m1, m2, op=OP.subtract))
            s_sig = sop(lambda E: E.activation(w1s, dd, AF.Sigmoid), waits=[("V", v_d2)])
            vop(lambda E: E.tensor_scalar(w2s, w1s, -1.0, 1.0, op0=OP.mult, op1=OP.add),
                waits=[("S", s_sig)])
            vop(lambda E: E.tensor_scalar(eq1, eq1, w1s, None, op0=OP.mult))
            vop(lambda E: E.tensor_scalar(eq2, eq2, w2s, None, op0=OP.mult))
            vop(lambda E, tb=tb: E.tensor_tensor(wtok[:, tb * 8:(tb + 1) * 8],
                                                 eq1, eq2, op=OP.add))
        v_prologue = S.nv
        t_prologue = t_logit_last

        dx(lambda E: E.dma_start(
            out=A(BIG, 8192, [[16384, 128], [1, 4096]]),
            in_=A(ramp_t, 0, [[4096, 128], [1, 4096]])), "R",
           waits=[("T", t_prologue)])
        v_ramp = 16

        fg = 0
        yc = 0
        silu_clk = {}
        gmul_clk = {}
        sttv = {}
        tclk_y = {}
        freeA = {}
        freeB = {}
        wA = {}
        wB = {}
        kt_clk = [None] * NC_DM
        kunit_next = 0
        extra_h1 = [0]
        extra_h3 = [0]
        v_init = [0, 0]
        YBUF = [YA, YB]

        def emit_kunit(c):
            v_x = vop(lambda E, c=c: E.tensor_tensor(
                A(BIG, 12288, [[16384, 128], [1, 4096]]),
                A(pt_sb, c * DS, [[NC_DM * DS, 128], [0, LL], [1, DS]]),
                A(BIG, 8192, [[16384, 128], [1, 4096]]), op=OP.mult),
                waits=[("R", 16), ("V", v_ptdone), ("V", v_bct),
                       ("T", t_prologue)])
            s_e = sop(lambda E: E.activation(
                A(BIG, 12288, [[16384, 128], [1, 4096]]),
                A(BIG, 12288, [[16384, 128], [1, 4096]]), AF.Exp),
                waits=[("V", v_x)])
            vop(lambda E, c=c: E.tensor_tensor(
                A(BIG, 12288, [[16384, 128], [1, 4096]]),
                A(BIG, 12288, [[16384, 128], [1, 4096]]),
                A(bct, c * DS, [[NC_DM * DS, 128], [0, LL], [1, DS]]), op=OP.mult),
                waits=[("S", s_e)])
            kt_clk[c] = vop(lambda E, c=c: E.tensor_reduce(
                kt[:, c * LL:(c + 1) * LL],
                A(BIG, 12288, [[16384, 128], [64, LL], [1, DS]]),
                axis=AX.X, op=OP.add))

        def emit_conv(s):
            conv_ev = {}
            add_clk = {}
            for c in range(NC_DM):
                dg_clk = {}
                cmm_clk = {}
                for l in range(LL):
                    w = [("V", kt_clk[c])]
                    if l >= 2:
                        w.append(("T", cmm_clk[l - 2]))
                    dg_clk[l] = vop(lambda E, c=c, l=l: E.tensor_scalar(
                        A(diag, (l % 2) * 128, [[256, 128], [1, 128]]),
                        ident[:, :], kt[:, c * LL + l:c * LL + l + 1], None,
                        op0=OP.mult), waits=w)
                    tw = [("V", dg_clk[l])]
                    if l == 0 and c > 0:
                        tw.append(("V", conv_ev[c - 1]))
                    elif l == 0:
                        tw.append(("S", silu_clk[fg - 1]))
                    for tc in range(2):
                        last = (tc == 1)
                        top(lambda E, c=c, l=l, tc=tc, s=s: E.matmul(
                            H1[:, tc * 512:(tc + 1) * 512],
                            A(diag, (l % 2) * 128, [[256, 128], [1, 128]]),
                            xnf[:, c * CW + s * TSUP + tc * 512 + 63 - l:
                                c * CW + s * TSUP + tc * 512 + 63 - l + 512],
                            start=(l == 0), stop=(l == LL - 1)),
                            waits=(tw if tc == 0 else ()), inc=last)
                    cmm_clk[l] = S.nt
                conv_ev[c] = vop(lambda E, c=c, s=s: E.scalar_tensor_tensor(
                    convb[:, :],
                    xnf[:, c * CW + PADL + s * TSUP:c * CW + PADL + (s + 1) * TSUP],
                    d_sb[:, c:c + 1], H1[:, :], op0=OP.mult, op1=OP.add),
                    waits=[("T", cmm_clk[LL - 1])])
                tw = [("V", conv_ev[c])]
                if c > 0:
                    tw.append(("V", add_clk[(c - 1, 7)]))
                for tb in range(8):
                    top(lambda E, c=c, tb=tb: E.transpose(
                        H3[:, tb * 128:(tb + 1) * 128],
                        convb[:, tb * 128:(tb + 1) * 128], ident[:, :]),
                        waits=(tw if tb == 0 else ()), inc=(tb == 7))
                t_ctr = S.nt
                for tb in range(8):
                    add_clk[(c, tb)] = vop(lambda E, c=c, tb=tb: E.tensor_tensor(
                        BIG[:, tb * 1024 + c * 128:tb * 1024 + (c + 1) * 128],
                        BIG[:, tb * 1024 + c * 128:tb * 1024 + (c + 1) * 128],
                        H3[:, tb * 128:(tb + 1) * 128], op=OP.add),
                        waits=[("T", t_ctr), v_init[s]])
            extra_h1[0] = conv_ev[NC_DM - 1]
            extra_h3[0] = add_clk[(NC_DM - 1, 7)]

        for s in range(NSUP):
            dx(lambda E, s=s: E.dma_start(
                out=A(BIG, 0, [[16384, 128], [1024, 8], [1, 1024]]),
                in_=A(u_t, s * TSUP * DM, [[1024, 128], [131072, 8], [1, 1024]])),
               "I0" if s == 0 else "I1",
               waits=([("T", t_prologue)] if s == 0 else ()))
            v_init[s] = ("I0" if s == 0 else "I1", 16)
            for e in range(NE):
                for g in range(NG):
                    segi = (s * NE + e) * NG + g
                    for j in range(4):
                        gsA = segi * 4 + j
                        pA = gsA % RA
                        fcol = g * 1024 + j * 256
                        if gsA >= RA:
                            S.wait("gpsimd", "T", freeA[gsA - RA])
                        dow(lambda E, e=e, fcol=fcol, pA=pA: E.dma_start(
                            out=A(w1r, pA * 2048, [[RA * 2048, 128], [256, 8], [1, 256]]),
                            in_=A(w1_t, e * DM * FF + fcol,
                                  [[FF, 128], [128 * FF, 8], [1, 256]])))
                        dow(lambda E, e=e, fcol=fcol, pA=pA: E.dma_start(
                            out=A(w3r, pA * 2048, [[RA * 2048, 128], [256, 8], [1, 256]]),
                            in_=A(w3_t, e * DM * FF + fcol,
                                  [[FF, 128], [128 * FF, 8], [1, 256]])))
                        gsB = segi * 4 + j
                        pB = gsB % RB
                        if gsB >= RB:
                            S.wait("gpsimd", "T", freeB.get((gsB - RB) // 4, 0) or 0)
                        dow(lambda E, e=e, fcol=fcol, pB=pB: E.dma_start(
                            out=A(w2r, pB * 2048, [[RB * 2048, 128], [1024, 2], [1, 1024]]),
                            in_=A(w2_t, e * FF * DM + fcol * DM,
                                  [[DM, 128], [128 * DM, 2], [1, 1024]])))
                    S.wait("gpsimd", "DW", S.ndw)
                    S.emit("gpsimd", lambda E, sems: E.sem_inc(sems["GF"], 1))
                    for fl in range(GF):
                        gsA = segi * 4 + fl // 2
                        pA = gsA % RA
                        scol = pA * 2048 + (fl % 2) * 128
                        w = [("GF", segi + 1), ("V", v_prologue)]
                        if fg > 0:
                            w.append(("S", silu_clk[fg - 1]))
                        if extra_h1[0]:
                            w.append(("V", extra_h1[0])); extra_h1[0] = 0
                        for k in range(NC_DM):
                            for tc in range(2):
                                top(lambda E, k=k, tc=tc, scol=scol, s=s: E.matmul(
                                    H1[:, tc * 512:(tc + 1) * 512],
                                    w1r[:, scol + k * 256:scol + k * 256 + 128],
                                    xnf[:, k * CW + PADL + s * TSUP + tc * 512:
                                        k * CW + PADL + s * TSUP + tc * 512 + 512],
                                    start=(k == 0), stop=(k == NC_DM - 1)),
                                    waits=(w if (k == 0 and tc == 0) else ()),
                                    inc=(k == NC_DM - 1 and tc == 1))
                        t_h1 = S.nt
                        w = []
                        if fg > 0:
                            w.append(("V", gmul_clk[fg - 1]))
                        if extra_h3[0]:
                            w.append(("V", extra_h3[0])); extra_h3[0] = 0
                        for k in range(NC_DM):
                            for tc in range(2):
                                top(lambda E, k=k, tc=tc, scol=scol, s=s: E.matmul(
                                    H3[:, tc * 512:(tc + 1) * 512],
                                    w3r[:, scol + k * 256:scol + k * 256 + 128],
                                    xnf[:, k * CW + PADL + s * TSUP + tc * 512:
                                        k * CW + PADL + s * TSUP + tc * 512 + 512],
                                    start=(k == 0), stop=(k == NC_DM - 1)),
                                    waits=(w if (k == 0 and tc == 0) else ()),
                                    inc=(k == NC_DM - 1 and tc == 1))
                        t_h3 = S.nt
                        if fl == 1 or fl == 3 or fl == 5 or fl == 7:
                            freeA[gsA] = t_h3
                        w = [("T", t_h1)]
                        if fg >= 2:
                            w.append(("V", gmul_clk[fg - 2]))
                        silu_clk[fg] = sop(lambda E, fl=fl: E.activation(
                            s_sb[:, (fl % 2) * TSUP:(fl % 2 + 1) * TSUP],
                            H1[:, :], AF.Silu), waits=w)
                        w = [("T", t_h3), ("S", silu_clk[fg])]
                        if yc > 0:
                            w.append(("T", tclk_y.get(yc - 1, 0) or 0))
                        gmul_clk[fg] = vop(lambda E, fl=fl: E.tensor_tensor(
                            g_sb[:, fl * TSUP:(fl + 1) * TSUP],
                            s_sb[:, (fl % 2) * TSUP:(fl % 2 + 1) * TSUP],
                            H3[:, :], op=OP.mult), waits=w)
                        if s == 0 and e == 0 and fl % 4 == 3:
                            nonlocal_c = (g * 8 + fl) // 4
                            if nonlocal_c < NC_DM and kt_clk[nonlocal_c] is None:
                                emit_kunit(nonlocal_c)
                        fg += 1
                    for tb in range(8):
                        buf = YBUF[yc % 2]
                        w = [("V", gmul_clk[fg - 1]), ("GF", segi + 1)]
                        if yc >= 2:
                            w.append(("V", sttv[yc - 2]))
                        for fl in range(GF):
                            pB = (segi * 4 + fl // 2) % RB
                            for dmh in range(2):
                                top(lambda E, fl=fl, dmh=dmh, tb=tb, pB=pB, buf=buf:
                                    E.matmul(
                                        buf[:, dmh * 512:(dmh + 1) * 512],
                                        g_sb[:, fl * TSUP + tb * 128:fl * TSUP + (tb + 1) * 128],
                                        w2r[:, pB * 2048 + (fl % 2) * 1024 + dmh * 512:
                                            pB * 2048 + (fl % 2) * 1024 + dmh * 512 + 512],
                                        start=(fl == 0), stop=(fl == GF - 1)),
                                    waits=(w if (fl == 0 and dmh == 0) else ()),
                                    inc=(fl == GF - 1 and dmh == 1))
                        tclk_y[yc] = S.nt
                        sttv[yc] = vop(lambda E, tb=tb, e=e, s=s, buf=buf:
                                       E.scalar_tensor_tensor(
                                           BIG[:, tb * 1024:(tb + 1) * 1024],
                                           buf[:, :],
                                           wtok[:, (s * 8 + tb) * 8 + e:(s * 8 + tb) * 8 + e + 1],
                                           BIG[:, tb * 1024:(tb + 1) * 1024],
                                           op0=OP.mult, op1=OP.add),
                                       waits=[("T", tclk_y[yc]), v_init[s]])
                        yc += 1
                    freeB[segi] = tclk_y[yc - 1]
                if e == 0:
                    emit_conv(s)
            dx(lambda E, s=s: E.dma_start(
                out=A(out_t, s * TSUP * DM, [[1024, 128], [131072, 8], [1, 1024]]),
                in_=A(BIG, 0, [[16384, 128], [1024, 8], [1, 1024]])), "O",
               waits=[("V", sttv[yc - 1])])
        S.wait("sync", "O", 32)

        with nc.Block() as block:
            @block.sync
            def _(E):
                for fn in S.ops["sync"]:
                    fn(E, sem_handles)

            @block.gpsimd
            def _(E):
                for fn in S.ops["gpsimd"]:
                    fn(E, sem_handles)

            @block.scalar
            def _(E):
                for fn in S.ops["scalar"]:
                    fn(E, sem_handles)

            @block.vector
            def _(E):
                for fn in S.ops["vector"]:
                    fn(E, sem_handles)

            @block.tensor
            def _(E):
                for fn in S.ops["tensor"]:
                    fn(E, sem_handles)
    return nc


def _layouts(inputs):
    Wg = np.ascontiguousarray(inputs["L0_ffn_Wg"], dtype=np.float32)
    wg2 = np.ascontiguousarray(Wg.reshape(8, 128, 8).transpose(1, 0, 2).reshape(128, 64))
    d2 = np.ascontiguousarray(np.asarray(inputs["L0_attn_D"], np.float32).reshape(8, 128).T)
    ldt2 = np.ascontiguousarray(
        np.asarray(inputs["L0_attn_log_dt"], np.float32).reshape(8, 128).T)
    ident = np.eye(128, dtype=np.float32)
    ramp = np.tile(np.repeat(np.arange(LL, dtype=np.float32), DS)[None, :], (128, 1))
    ramp = np.ascontiguousarray(ramp)
    return wg2, d2, ldt2, ident, ramp


def kernel(L0_attn_u, L0_attn_A, L0_attn_B, L0_attn_C, L0_attn_D, L0_attn_log_dt,
           L0_ffn_Wg, L0_ffn_W1, L0_ffn_W3, L0_ffn_W2):
    u = np.ascontiguousarray(np.asarray(L0_attn_u, np.float32))
    W1 = np.ascontiguousarray(np.asarray(L0_ffn_W1, np.float32))
    W3 = np.ascontiguousarray(np.asarray(L0_ffn_W3, np.float32))
    W2 = np.ascontiguousarray(np.asarray(L0_ffn_W2, np.float32))
    Am = np.ascontiguousarray(np.asarray(L0_attn_A, np.float32).reshape(1, DS))
    Bm = np.ascontiguousarray(np.asarray(L0_attn_B, np.float32))
    Cm = np.ascontiguousarray(np.asarray(L0_attn_C, np.float32))
    ins = {"L0_ffn_Wg": L0_ffn_Wg, "L0_attn_D": L0_attn_D,
           "L0_attn_log_dt": L0_attn_log_dt}
    wg2, d2, ldt2, ident, ramp = _layouts(ins)
    nc = build_nc()
    in_maps = []
    for b in range(NCORES):
        in_maps.append({
            "u": np.ascontiguousarray(u[b]),
            "w1": W1, "w3": W3, "w2": W2, "wg2": wg2,
            "a_v": Am, "b_m": Bm, "c_m": Cm, "d2": d2, "logdt2": ldt2,
            "ident": ident, "ramp": ramp,
        })
    res = run_bass_kernel_spmd(nc, in_maps, core_ids=list(range(NCORES)))
    out = np.stack([np.asarray(res.results[b]["out"]) for b in range(NCORES)], axis=0)
    return out


if __name__ == "__main__":
    nc = build_nc()
    print("build ok")


# version 03
# speedup vs baseline: 1.2650x; 1.2650x over previous
import numpy as np
import concourse.bass as bass
import concourse.mybir as mybir
from concourse.bass_utils import run_bass_kernel_spmd

F32 = mybir.dt.float32
BF16 = mybir.dt.bfloat16
AF = mybir.ActivationFunctionType
OP = mybir.AluOpType
AX = mybir.AxisListType

NCORES = 8
T = 2048
DM = 1024
DS = 64
LL = 64
FF = 4096
NE = 8
TH = 1024
NC_DM = DM // 128
NG = 4
GF = 8
RA = 6
RB = 5
PADL, PADR = 32, 31
CW = PADL + T + PADR


class _Sched:
    def __init__(self):
        self.ops = {k: [] for k in ("sync", "gpsimd", "scalar", "vector", "tensor")}
        self.nv = 0
        self.ns = 0
        self.nt = 0
        self.ndw = 0
        self.ngf = 0
        self.lw = {k: {} for k in self.ops}

    def emit(self, eng, fn):
        self.ops[eng].append(fn)

    def wait(self, eng, key, val):
        if val is None or val <= 0:
            return
        if self.lw[eng].get(key, 0) >= val:
            return
        self.lw[eng][key] = val
        self.emit(eng, lambda E, sems, k=key, v=val: E.wait_ge(sems[k], v))


def build_nc():
    nc = bass.Bass("TRN2", target_bir_lowering=False, detect_race_conditions=False)

    u_t = nc.dram_tensor("u", [T, DM], F32, kind="ExternalInput")
    w1_t = nc.dram_tensor("w1", [NE, DM, FF], F32, kind="ExternalInput")
    w3_t = nc.dram_tensor("w3", [NE, DM, FF], F32, kind="ExternalInput")
    w2_t = nc.dram_tensor("w2", [NE, FF, DM], F32, kind="ExternalInput")
    wg_t = nc.dram_tensor("wg2", [128, 64], F32, kind="ExternalInput")
    a_t = nc.dram_tensor("a_v", [1, DS], F32, kind="ExternalInput")
    b_t = nc.dram_tensor("b_m", [DS, DM], F32, kind="ExternalInput")
    c_t = nc.dram_tensor("c_m", [DM, DS], F32, kind="ExternalInput")
    d_t = nc.dram_tensor("d2", [128, NC_DM], F32, kind="ExternalInput")
    ldt_t = nc.dram_tensor("logdt2", [128, NC_DM], F32, kind="ExternalInput")
    id_t = nc.dram_tensor("ident", [128, 128], F32, kind="ExternalInput")
    out_t = nc.dram_tensor("out", [T, DM], F32, kind="ExternalOutput")

    from contextlib import ExitStack
    with ExitStack() as ctx:
        EC = ctx.enter_context
        BIG = EC(nc.sbuf_tensor("BIG", [128, 16384], F32))
        xnf = EC(nc.sbuf_tensor("xnf", [128, NC_DM * CW], BF16))
        w1r = EC(nc.sbuf_tensor("w1r", [128, RA * 2048], BF16))
        w3r = EC(nc.sbuf_tensor("w3r", [128, RA * 2048], BF16))
        w2r = EC(nc.sbuf_tensor("w2r", [128, RB * 2048], BF16))
        g_sb = EC(nc.sbuf_tensor("g_sb", [128, GF * TH], BF16))
        s_sb = EC(nc.sbuf_tensor("s_sb", [128, 2 * TH], BF16))
        convb = EC(nc.sbuf_tensor("convb", [128, TH], F32))
        kscr = EC(nc.sbuf_tensor("kscr", [128, 1024], F32))
        kt = EC(nc.sbuf_tensor("kt", [128, NC_DM * LL], F32))
        bct = EC(nc.sbuf_tensor("bct", [128, NC_DM * DS], F32))
        pt_sb = EC(nc.sbuf_tensor("pt_sb", [128, NC_DM * DS], F32))
        c_sb = EC(nc.sbuf_tensor("c_sb", [128, NC_DM * DS], F32))
        a_b = EC(nc.sbuf_tensor("a_b", [128, DS], F32))
        dtc = EC(nc.sbuf_tensor("dtc", [128, NC_DM], F32))
        d_sb = EC(nc.sbuf_tensor("d_sb", [128, NC_DM], F32))
        ldt_sb = EC(nc.sbuf_tensor("ldt_sb", [128, NC_DM], F32))
        wg_sb = EC(nc.sbuf_tensor("wg_sb", [128, 64], F32))
        ident = EC(nc.sbuf_tensor("identt", [128, 128], F32))
        diag = EC(nc.sbuf_tensor("diag", [128, 2 * 128], BF16))
        csr = EC(nc.sbuf_tensor("csr", [1, DS], F32))
        csc = EC(nc.sbuf_tensor("csc", [DS, 1], F32))
        ones_c = EC(nc.sbuf_tensor("ones_c", [128, 1], F32))
        ones_r = EC(nc.sbuf_tensor("ones_r", [1, 128], F32))
        stats = EC(nc.sbuf_tensor("stats", [128, 12], F32))
        aggr16 = EC(nc.sbuf_tensor("aggr16", [128, 32], F32))
        rstd16 = EC(nc.sbuf_tensor("rstd16", [128, 16], F32))
        sq16 = EC(nc.sbuf_tensor("sq16", [128, 16], F32))
        lgb = EC(nc.sbuf_tensor("lgb", [128, 8], F32))
        eps_sb = EC(nc.sbuf_tensor("eps_sb", [128, 1], F32))
        rsc = EC(nc.sbuf_tensor("rsc", [128, 48], F32))
        wtok = EC(nc.sbuf_tensor("wtok", [128, 128], F32))
        b_sb = convb
        tmp32 = kscr
        H1 = EC(nc.psum_tensor("H1", [128, 1024], F32))
        H3 = EC(nc.psum_tensor("H3", [128, 1024], F32))
        YA = EC(nc.psum_tensor("YA", [128, 1024], F32))
        YB = EC(nc.psum_tensor("YB", [128, 1024], F32))
        sem_handles = {}
        for k in ("T", "V", "S", "DW", "GFS", "U", "P", "I0", "O"):
            sem_handles[k] = EC(nc.semaphore("sem" + k))

        S = _Sched()

        def vop(fn, waits=()):
            for k, v in waits:
                S.wait("vector", k, v)
            S.wait("vector", "V", S.nv)
            S.emit("vector", lambda E, sems, f=fn: f(E).then_inc(sems["V"], 1))
            S.nv += 1
            return S.nv

        def sop(fn, waits=()):
            for k, v in waits:
                S.wait("scalar", k, v)
            S.wait("scalar", "S", S.ns)
            S.emit("scalar", lambda E, sems, f=fn: f(E).then_inc(sems["S"], 1))
            S.ns += 1
            return S.ns

        def top(fn, waits=(), inc=False):
            for k, v in waits:
                S.wait("tensor", k, v)
            if inc:
                S.emit("tensor", lambda E, sems, f=fn: f(E).then_inc(sems["T"], 1))
                S.nt += 1
                return S.nt
            S.emit("tensor", lambda E, sems, f=fn: f(E))
            return None

        def dx(fn, sem, waits=()):
            for k, v in waits:
                S.wait("sync", k, v)
            S.emit("sync", lambda E, sems, f=fn, sm=sem: f(E).then_inc(sems[sm], 16))

        def dow(fn, waits=()):
            for k, v in waits:
                S.wait("gpsimd", k, v)
            S.emit("gpsimd", lambda E, sems, f=fn: f(E).then_inc(sems["DW"], 16))
            S.ndw += 16

        A = bass.AP

        dx(lambda E: E.dma_start(
            out=A(BIG, 0, [[16384, 128], [1024, 16], [1, 1024]]),
            in_=A(u_t, 0, [[1024, 128], [131072, 16], [1, 1024]])), "U")
        dx(lambda E: E.dma_start(out=ident[:, :], in_=id_t[:, :]), "P")
        dx(lambda E: E.dma_start(
            out=A(c_sb, 0, [[512, 128], [64, 8], [1, 64]]),
            in_=A(c_t, 0, [[64, 128], [8192, 8], [1, 64]])), "P")
        dx(lambda E: E.dma_start(out=b_sb[0:DS, 0:DM], in_=b_t[:, :]), "P")
        dx(lambda E: E.dma_start(out=A(a_b, 0, [[64, 1], [1, 64]]),
                                 in_=a_t[:, :]), "P")
        dx(lambda E: E.dma_start(out=ldt_sb[:, :], in_=ldt_t[:, :]), "P")
        dx(lambda E: E.dma_start(out=d_sb[:, :], in_=d_t[:, :]), "P")
        dx(lambda E: E.dma_start(out=wg_sb[:, :], in_=wg_t[:, :]), "P")
        v_par = 7 * 16

        vop(lambda E: E.memset(ones_c[:, :], 1.0))
        vop(lambda E: E.memset(eps_sb[:, :], 1e-5))
        vop(lambda E: E.memset(ones_r[:, :], 1.0))
        vop(lambda E: E.memset(A(xnf, 0, [[NC_DM * CW, 128], [CW, NC_DM], [1, PADL]]), 0.0))
        v_const = vop(lambda E: E.memset(
            A(xnf, PADL + T, [[NC_DM * CW, 128], [CW, NC_DM], [1, PADR]]), 0.0))

        for cp in range(NC_DM):
            t_csum = top(lambda E, cp=cp: E.matmul(
                YA[0:1, cp * 64:(cp + 1) * 64], ones_c[:, 0:1],
                c_sb[:, cp * 64:(cp + 1) * 64], start=True, stop=True),
                waits=([("P", v_par), ("V", v_const)] if cp == 0 else ()),
                inc=(cp == NC_DM - 1))
        v_csr = vop(lambda E: E.tensor_reduce(
            csr[0:1, 0:DS], A(YA, 0, [[1024, 1], [1, DS], [64, NC_DM]]),
            axis=AX.X, op=OP.add), waits=[("T", t_csum)])
        t_csc = top(lambda E: E.transpose(YB[0:DS, 0:1], csr[0:1, 0:DS], ident[0:1, 0:1]),
                    waits=[("V", v_csr)], inc=True)
        v_csc = vop(lambda E: E.tensor_copy(csc[:, :], YB[0:DS, 0:1]), waits=[("T", t_csc)])
        v_bc = vop(lambda E: E.tensor_scalar(b_sb[0:DS, 0:DM], b_sb[0:DS, 0:DM],
                                             csc[:, 0:1], None, op0=OP.mult))
        for cp in range(NC_DM):
            t_bct = top(lambda E, cp=cp: E.transpose(
                YA[:, cp * 64:(cp + 1) * 64], b_sb[0:DS, cp * 128:(cp + 1) * 128],
                ident[0:DS, 0:DS]),
                waits=([("V", v_bc)] if cp == 0 else ()), inc=(cp == NC_DM - 1))
        v_bct = vop(lambda E: E.tensor_copy(bct[:, :], YA[:, 0:512]), waits=[("T", t_bct)])
        t_ab = top(lambda E: E.matmul(YB[:, 0:DS], ones_r[0:1, 0:128],
                                      A(a_b, 0, [[64, 1], [1, 64]]), start=True, stop=True),
                   waits=[("V", v_csc)], inc=True)
        v_ab = vop(lambda E: E.tensor_copy(a_b[:, :], YB[:, 0:DS]), waits=[("T", t_ab)])
        s_dtc = sop(lambda E: E.activation(dtc[:, :], ldt_sb[:, :], AF.Exp),
                    waits=[("P", v_par)])
        for c in range(NC_DM):
            v_pt = vop(lambda E, c=c: E.tensor_scalar(
                pt_sb[:, c * DS:(c + 1) * DS], a_b[:, :], dtc[:, c:c + 1], None,
                op0=OP.mult), waits=[("S", s_dtc)])
        v_ptdone = v_pt

        for tb in range(16):
            vop(lambda E, tb=tb: E.bn_stats(stats[:, 0:6], BIG[:, tb * 1024:tb * 1024 + 512]),
                waits=[("U", 16)])
            vop(lambda E, tb=tb: E.bn_stats(stats[:, 6:12],
                                            BIG[:, tb * 1024 + 512:tb * 1024 + 1024]))
            vop(lambda E, tb=tb: E.bn_aggr(aggr16[:, 2 * tb:2 * tb + 2], stats[:, 0:12]))
        v_aggr = S.nv
        for tb in range(16):
            sop(lambda E, tb=tb: E.activation(sq16[:, tb:tb + 1],
                                              aggr16[:, 2 * tb + 1:2 * tb + 2],
                                              AF.Sqrt, bias=eps_sb[:, 0:1]),
                waits=[("V", v_aggr)])
        s_sq = S.ns
        for tb in range(16):
            vop(lambda E, tb=tb: E.reciprocal(rstd16[:, tb:tb + 1], sq16[:, tb:tb + 1]),
                waits=[("S", s_sq)])
            vop(lambda E, tb=tb: E.tensor_scalar(
                BIG[:, tb * 1024:(tb + 1) * 1024], BIG[:, tb * 1024:(tb + 1) * 1024],
                aggr16[:, 2 * tb:2 * tb + 1], rstd16[:, tb:tb + 1],
                op0=OP.subtract, op1=OP.mult))
        v_lnapply = S.nv

        ev_half = {}
        ev_clk = {}
        mm_clk = {}
        lg_clk = {}
        for tb in range(16):
            t_half = {}
            for half in range(2):
                for c in range(4 * half, 4 * half + 4):
                    w = [("V", v_lnapply)]
                    if tb > 0:
                        w.append(("V", ev_half[(tb - 1, half)]))
                    elif half == 0 and c == 0:
                        w.append(("V", v_ab))
                    top(lambda E, tb=tb, c=c: E.transpose(
                        YB[:, c * 128:(c + 1) * 128],
                        BIG[:, tb * 1024 + c * 128:tb * 1024 + (c + 1) * 128],
                        ident[:, :]), waits=(w if c == 4 * half else ()),
                        inc=(c == 4 * half + 3))
                t_half[half] = S.nt
            for half in range(2):
                for c in range(4 * half, 4 * half + 4):
                    vop(lambda E, tb=tb, c=c: E.tensor_copy(
                        xnf[:, c * CW + PADL + tb * 128:c * CW + PADL + (tb + 1) * 128],
                        YB[:, c * 128:(c + 1) * 128]), waits=[("T", t_half[half])])
                    ev_clk[(tb, c)] = vop(lambda E, tb=tb, c=c: E.tensor_copy(
                        tmp32[:, c * 128:(c + 1) * 128],
                        YB[:, c * 128:(c + 1) * 128]))
                ev_half[(tb, half)] = ev_clk[(tb, 4 * half + 3)]
            for c in range(NC_DM):
                w = [("V", ev_clk[(tb, c)]), ("P", v_par)]
                if c == 0 and tb > 0:
                    w.append(("V", lg_clk[tb - 1]))
                elif c == 0:
                    w.append(("V", v_bct))
                mm_clk[(tb, c)] = top(lambda E, tb=tb, c=c: E.matmul(
                    YA[:, 0:8], tmp32[:, c * 128:(c + 1) * 128],
                    wg_sb[:, c * 8:(c + 1) * 8], start=(c == 0), stop=(c == NC_DM - 1)),
                    waits=w, inc=True)
            t_logit_last = mm_clk[(tb, NC_DM - 1)]
            m1 = rsc[:, 0:1]; m2 = rsc[:, 1:2]; dd = rsc[:, 2:3]
            w1s = rsc[:, 3:4]; w2s = rsc[:, 4:5]
            l2 = rsc[:, 8:16]; eq1 = rsc[:, 16:24]; eq2 = rsc[:, 24:32]
            lg_clk[tb] = vop(lambda E: E.tensor_copy(lgb[:, :], YA[:, 0:8]),
                             waits=[("T", t_logit_last)])
            vop(lambda E: E.reduce_max(m1, lgb[:, :], axis=AX.X))
            vop(lambda E: E.tensor_scalar(eq1, lgb[:, :], m1, None, op0=OP.is_equal))
            vop(lambda E: E.scalar_tensor_tensor(l2, eq1, -1e30, lgb[:, :],
                                                 op0=OP.mult, op1=OP.add))
            vop(lambda E: E.reduce_max(m2, l2, axis=AX.X))
            vop(lambda E: E.tensor_scalar(eq2, l2, m2, None, op0=OP.is_equal))
            v_d2 = vop(lambda E: E.tensor_tensor(dd, m1, m2, op=OP.subtract))
            s_sig = sop(lambda E: E.activation(w1s, dd, AF.Sigmoid), waits=[("V", v_d2)])
            vop(lambda E: E.tensor_scalar(w2s, w1s, -1.0, 1.0, op0=OP.mult, op1=OP.add),
                waits=[("S", s_sig)])
            vop(lambda E: E.tensor_scalar(eq1, eq1, w1s, None, op0=OP.mult))
            vop(lambda E: E.tensor_scalar(eq2, eq2, w2s, None, op0=OP.mult))
            vop(lambda E, tb=tb: E.tensor_tensor(wtok[:, tb * 8:(tb + 1) * 8],
                                                 eq1, eq2, op=OP.add))
        v_prologue = S.nv
        t_prologue = t_logit_last

        dx(lambda E: E.dma_start(
            out=A(BIG, 0, [[16384, 128], [1024, 16], [1, 1024]]),
            in_=A(u_t, 0, [[1024, 128], [131072, 16], [1, 1024]])), "I0",
           waits=[("T", t_prologue)])

        fg = 0
        yc = 0
        silu_clk = {}
        gmul_clk = {}
        sttv = {}
        tclk_y = {}
        freeA = {}
        freeB = {}
        kt_clk = [None] * NC_DM
        kred = [0]
        extra_h1 = [0]
        extra_h3 = [0]
        extra_y = [0]
        YBUF = [YA, YB]

        def emit_kunit(c):
            for p4 in range(4):
                w = [("T", t_prologue)] if (c == 0 and p4 == 0) else []
                if kred[0]:
                    w.append(("V", kred[0]))
                for lj in range(16):
                    l = p4 * 16 + lj
                    s_e = sop(lambda E, c=c, l=l, lj=lj: E.activation(
                        kscr[:, lj * 64:(lj + 1) * 64], pt_sb[:, c * DS:(c + 1) * DS],
                        AF.Exp, scale=float(l)), waits=w)
                    w = []
                v_m = vop(lambda E: E.tensor_tensor(
                    kscr[:, :], kscr[:, :],
                    A(bct, c * DS, [[NC_DM * DS, 128], [0, 16], [1, DS]]), op=OP.mult),
                    waits=[("S", s_e), ("V", v_bct), ("V", v_ptdone)])
                kt_clk[c] = vop(lambda E, c=c, p4=p4: E.tensor_reduce(
                    kt[:, c * LL + p4 * 16:c * LL + p4 * 16 + 16],
                    A(kscr, 0, [[1024, 128], [64, 16], [1, DS]]),
                    axis=AX.X, op=OP.add))
                kred[0] = kt_clk[c]

        def emit_conv():
            conv_ev = {}
            add_clk = {}
            v_ylast = sttv[yc - 1]
            for c in range(NC_DM):
                dg_clk = {}
                cmm_clk = {}
                for l in range(LL):
                    w = [("V", kt_clk[c])]
                    if l >= 2:
                        w.append(("T", cmm_clk[l - 2]))
                    dg_clk[l] = vop(lambda E, c=c, l=l: E.tensor_scalar(
                        A(diag, (l % 2) * 128, [[256, 128], [1, 128]]),
                        ident[:, :], kt[:, c * LL + l:c * LL + l + 1], None,
                        op0=OP.mult), waits=w)
                    tw = [("V", dg_clk[l])]
                    if l == 0 and c > 0:
                        tw.append(("V", conv_ev[(c - 1, 1)]))
                    elif l == 0:
                        tw.append(("S", silu_clk[fg - 1]))
                        tw.append(("V", gmul_clk[fg - 1]))
                    first = True
                    for h in range(2):
                        PS = H1 if h == 0 else H3
                        for tc in range(2):
                            top(lambda E, c=c, l=l, tc=tc, h=h, PS=PS: E.matmul(
                                PS[:, tc * 512:(tc + 1) * 512],
                                A(diag, (l % 2) * 128, [[256, 128], [1, 128]]),
                                xnf[:, c * CW + h * TH + tc * 512 + 63 - l:
                                    c * CW + h * TH + tc * 512 + 63 - l + 512],
                                start=(l == 0), stop=(l == LL - 1)),
                                waits=(tw if first else ()),
                                inc=(h == 1 and tc == 1))
                            first = False
                    cmm_clk[l] = S.nt
                for h in range(2):
                    PS = H1 if h == 0 else H3
                    TS = YA if h == 0 else YB
                    conv_ev[(c, h)] = vop(lambda E, c=c, h=h, PS=PS:
                                          E.scalar_tensor_tensor(
                        convb[:, :],
                        xnf[:, c * CW + PADL + h * TH:c * CW + PADL + (h + 1) * TH],
                        d_sb[:, c:c + 1], PS[:, :], op0=OP.mult, op1=OP.add),
                        waits=[("T", cmm_clk[LL - 1])])
                    tw = [("V", conv_ev[(c, h)])]
                    if c > 0 or h > 0:
                        prev = (c, 0) if h == 1 else (c - 1, 1)
                        tw.append(("V", add_clk[(prev, 7)]))
                    else:
                        tw.append(("V", v_ylast))
                    for tb in range(8):
                        top(lambda E, tb=tb, TS=TS: E.transpose(
                            TS[:, tb * 128:(tb + 1) * 128],
                            convb[:, tb * 128:(tb + 1) * 128], ident[:, :]),
                            waits=(tw if tb == 0 else ()), inc=(tb == 7))
                    t_ctr = S.nt
                    for tb in range(8):
                        yt = h * 8 + tb
                        add_clk[((c, h), tb)] = vop(
                            lambda E, c=c, yt=yt, tb=tb, TS=TS: E.tensor_tensor(
                                BIG[:, yt * 1024 + c * 128:yt * 1024 + (c + 1) * 128],
                                BIG[:, yt * 1024 + c * 128:yt * 1024 + (c + 1) * 128],
                                TS[:, tb * 128:(tb + 1) * 128], op=OP.add),
                            waits=[("T", t_ctr), ("I0", 16)])
            extra_h1[0] = conv_ev[(NC_DM - 1, 0)]
            extra_h3[0] = conv_ev[(NC_DM - 1, 1)]
            extra_y[0] = add_clk[((NC_DM - 1, 1), 7)]

        for e in range(NE):
            for g in range(NG):
                segi = e * NG + g
                for j in range(4):
                    gsA = segi * 4 + j
                    pA = gsA % RA
                    pB = gsA % RB
                    fcol = g * 1024 + j * 256
                    if gsA >= RA:
                        S.wait("gpsimd", "T", freeA[gsA - RA])
                    dow(lambda E, e=e, fcol=fcol, pA=pA: E.dma_start(
                        out=A(w1r, pA * 2048, [[RA * 2048, 128], [256, 8], [1, 256]]),
                        in_=A(w1_t, e * DM * FF + fcol,
                              [[FF, 128], [128 * FF, 8], [1, 256]])))
                    dow(lambda E, e=e, fcol=fcol, pA=pA: E.dma_start(
                        out=A(w3r, pA * 2048, [[RA * 2048, 128], [256, 8], [1, 256]]),
                        in_=A(w3_t, e * DM * FF + fcol,
                              [[FF, 128], [128 * FF, 8], [1, 256]])))
                    if gsA >= RB:
                        S.wait("gpsimd", "T", freeB[gsA - RB])
                    dow(lambda E, e=e, fcol=fcol, pB=pB: E.dma_start(
                        out=A(w2r, pB * 2048, [[RB * 2048, 128], [1024, 2], [1, 1024]]),
                        in_=A(w2_t, e * FF * DM + fcol * DM,
                              [[DM, 128], [128 * DM, 2], [1, 1024]])))
                    S.wait("gpsimd", "DW", S.ndw)
                    S.emit("gpsimd", lambda E, sems: E.sem_inc(sems["GFS"], 1))
                    S.ngf += 1
                for h in range(2):
                    for fl in range(GF):
                        gsA = segi * 4 + fl // 2
                        pA = gsA % RA
                        scol = pA * 2048 + (fl % 2) * 128
                        w = [("GFS", segi * 4 + fl // 2 + 1), ("V", v_prologue)]
                        if fg > 0:
                            w.append(("S", silu_clk[fg - 1]))
                        if extra_h1[0]:
                            w.append(("V", extra_h1[0])); extra_h1[0] = 0
                        for k in range(NC_DM):
                            for tc in range(2):
                                top(lambda E, k=k, tc=tc, scol=scol, h=h: E.matmul(
                                    H1[:, tc * 512:(tc + 1) * 512],
                                    w1r[:, scol + k * 256:scol + k * 256 + 128],
                                    xnf[:, k * CW + PADL + h * TH + tc * 512:
                                        k * CW + PADL + h * TH + tc * 512 + 512],
                                    start=(k == 0), stop=(k == NC_DM - 1)),
                                    waits=(w if (k == 0 and tc == 0) else ()),
                                    inc=(k == NC_DM - 1 and tc == 1))
                        t_h1 = S.nt
                        w = []
                        if fg > 0:
                            w.append(("V", gmul_clk[fg - 1]))
                        if extra_h3[0]:
                            w.append(("V", extra_h3[0])); extra_h3[0] = 0
                        for k in range(NC_DM):
                            for tc in range(2):
                                top(lambda E, k=k, tc=tc, scol=scol, h=h: E.matmul(
                                    H3[:, tc * 512:(tc + 1) * 512],
                                    w3r[:, scol + k * 256:scol + k * 256 + 128],
                                    xnf[:, k * CW + PADL + h * TH + tc * 512:
                                        k * CW + PADL + h * TH + tc * 512 + 512],
                                    start=(k == 0), stop=(k == NC_DM - 1)),
                                    waits=(w if (k == 0 and tc == 0) else ()),
                                    inc=(k == NC_DM - 1 and tc == 1))
                        t_h3 = S.nt
                        if h == 1 and fl % 2 == 1:
                            freeA[gsA] = t_h3
                        w = [("T", t_h1)]
                        if fg >= 2:
                            w.append(("V", gmul_clk[fg - 2]))
                        silu_clk[fg] = sop(lambda E, fl=fl: E.activation(
                            s_sb[:, (fl % 2) * TH:(fl % 2 + 1) * TH],
                            H1[:, :], AF.Silu), waits=w)
                        w = [("T", t_h3), ("S", silu_clk[fg])]
                        if yc > 0:
                            w.append(("T", tclk_y.get(yc - 1, 0) or 0))
                        gmul_clk[fg] = vop(lambda E, fl=fl: E.tensor_tensor(
                            g_sb[:, fl * TH:(fl + 1) * TH],
                            s_sb[:, (fl % 2) * TH:(fl % 2 + 1) * TH],
                            H3[:, :], op=OP.mult), waits=w)
                        if e == 0 and h == 0 and fl % 4 == 3:
                            kc = (g * 8 + fl) // 4
                            if kc < NC_DM and kt_clk[kc] is None:
                                emit_kunit(kc)
                        fg += 1
                    for tb in range(8):
                        buf = YBUF[yc % 2]
                        yt = h * 8 + tb
                        w = [("V", gmul_clk[fg - 1]), ("GFS", segi * 4 + 4)]
                        if yc >= 2:
                            w.append(("V", sttv[yc - 2]))
                        if extra_y[0]:
                            w.append(("V", extra_y[0])); extra_y[0] = 0
                        for fl in range(GF):
                            pB = (segi * 4 + fl // 2) % RB
                            for dmh in range(2):
                                top(lambda E, fl=fl, dmh=dmh, tb=tb, pB=pB, buf=buf:
                                    E.matmul(
                                        buf[:, dmh * 512:(dmh + 1) * 512],
                                        g_sb[:, fl * TH + tb * 128:fl * TH + (tb + 1) * 128],
                                        w2r[:, pB * 2048 + (fl % 2) * 1024 + dmh * 512:
                                            pB * 2048 + (fl % 2) * 1024 + dmh * 512 + 512],
                                        start=(fl == 0), stop=(fl == GF - 1)),
                                    waits=(w if (fl == 0 and dmh == 0) else ()),
                                    inc=(fl == GF - 1 and dmh == 1))
                        tclk_y[yc] = S.nt
                        sttv[yc] = vop(lambda E, yt=yt, e=e, buf=buf:
                                       E.scalar_tensor_tensor(
                                           BIG[:, yt * 1024:(yt + 1) * 1024],
                                           buf[:, :],
                                           wtok[:, yt * 8 + e:yt * 8 + e + 1],
                                           BIG[:, yt * 1024:(yt + 1) * 1024],
                                           op0=OP.mult, op1=OP.add),
                                       waits=[("T", tclk_y[yc]), ("I0", 16)])
                        yc += 1
                for j in range(4):
                    freeB[segi * 4 + j] = tclk_y[yc - 1]
            if e == 0:
                emit_conv()
        dx(lambda E: E.dma_start(
            out=A(out_t, 0, [[1024, 128], [131072, 16], [1, 1024]]),
            in_=A(BIG, 0, [[16384, 128], [1024, 16], [1, 1024]])), "O",
           waits=[("V", sttv[yc - 1])])
        S.wait("sync", "O", 16)

        with nc.Block() as block:
            @block.sync
            def _(E):
                for fn in S.ops["sync"]:
                    fn(E, sem_handles)

            @block.gpsimd
            def _(E):
                for fn in S.ops["gpsimd"]:
                    fn(E, sem_handles)

            @block.scalar
            def _(E):
                for fn in S.ops["scalar"]:
                    fn(E, sem_handles)

            @block.vector
            def _(E):
                for fn in S.ops["vector"]:
                    fn(E, sem_handles)

            @block.tensor
            def _(E):
                for fn in S.ops["tensor"]:
                    fn(E, sem_handles)
    return nc


def _layouts(inputs):
    Wg = np.ascontiguousarray(np.asarray(inputs["L0_ffn_Wg"], np.float32))
    wg2 = np.ascontiguousarray(Wg.reshape(8, 128, 8).transpose(1, 0, 2).reshape(128, 64))
    d2 = np.ascontiguousarray(np.asarray(inputs["L0_attn_D"], np.float32).reshape(8, 128).T)
    ldt2 = np.ascontiguousarray(
        np.asarray(inputs["L0_attn_log_dt"], np.float32).reshape(8, 128).T)
    ident = np.eye(128, dtype=np.float32)
    return wg2, d2, ldt2, ident


def kernel(L0_attn_u, L0_attn_A, L0_attn_B, L0_attn_C, L0_attn_D, L0_attn_log_dt,
           L0_ffn_Wg, L0_ffn_W1, L0_ffn_W3, L0_ffn_W2):
    u = np.ascontiguousarray(np.asarray(L0_attn_u, np.float32))
    W1 = np.ascontiguousarray(np.asarray(L0_ffn_W1, np.float32))
    W3 = np.ascontiguousarray(np.asarray(L0_ffn_W3, np.float32))
    W2 = np.ascontiguousarray(np.asarray(L0_ffn_W2, np.float32))
    Am = np.ascontiguousarray(np.asarray(L0_attn_A, np.float32).reshape(1, DS))
    Bm = np.ascontiguousarray(np.asarray(L0_attn_B, np.float32))
    Cm = np.ascontiguousarray(np.asarray(L0_attn_C, np.float32))
    ins = {"L0_ffn_Wg": L0_ffn_Wg, "L0_attn_D": L0_attn_D,
           "L0_attn_log_dt": L0_attn_log_dt}
    wg2, d2, ldt2, ident = _layouts(ins)
    nc = build_nc()
    in_maps = []
    for b in range(NCORES):
        in_maps.append({
            "u": np.ascontiguousarray(u[b]),
            "w1": W1, "w3": W3, "w2": W2, "wg2": wg2,
            "a_v": Am, "b_m": Bm, "c_m": Cm, "d2": d2, "logdt2": ldt2,
            "ident": ident,
        })
    res = run_bass_kernel_spmd(nc, in_maps, core_ids=list(range(NCORES)))
    out = np.stack([np.asarray(res.results[b]["out"]) for b in range(NCORES)], axis=0)
    return out


if __name__ == "__main__":
    nc = build_nc()
    print("build ok")


# version 04
# speedup vs baseline: 1.3408x; 1.0599x over previous
import numpy as np
import concourse.bass as bass
import concourse.mybir as mybir
from concourse.bass_utils import run_bass_kernel_spmd

F32 = mybir.dt.float32
BF16 = mybir.dt.bfloat16
AF = mybir.ActivationFunctionType
OP = mybir.AluOpType
AX = mybir.AxisListType

NCORES = 8
T = 2048
DM = 1024
DS = 64
LL = 64
FF = 4096
NE = 8
TH = 1024
NC_DM = DM // 128
NG = 4
GF = 8
RA = 6
RB = 5
PADL, PADR = 32, 31
CW = PADL + T + PADR


class _Sched:
    def __init__(self):
        self.ops = {k: [] for k in ("sync", "gpsimd", "scalar", "vector", "tensor")}
        self.nv = 0
        self.ns = 0
        self.nt = 0
        self.ndw = 0
        self.ngf = 0
        self.lw = {k: {} for k in self.ops}

    def emit(self, eng, fn):
        self.ops[eng].append(fn)

    def wait(self, eng, key, val):
        if val is None or val <= 0:
            return
        if self.lw[eng].get(key, 0) >= val:
            return
        self.lw[eng][key] = val
        self.emit(eng, lambda E, sems, k=key, v=val: E.wait_ge(sems[k], v))


def build_nc():
    nc = bass.Bass("TRN2", target_bir_lowering=False, detect_race_conditions=False)

    u_t = nc.dram_tensor("u", [T, DM], F32, kind="ExternalInput")
    w1_t = nc.dram_tensor("w1", [NE, DM, FF], F32, kind="ExternalInput")
    w3_t = nc.dram_tensor("w3", [NE, DM, FF], F32, kind="ExternalInput")
    w2_t = nc.dram_tensor("w2", [NE, FF, DM], F32, kind="ExternalInput")
    wg_t = nc.dram_tensor("wg2", [128, 64], F32, kind="ExternalInput")
    a_t = nc.dram_tensor("a_v", [1, DS], F32, kind="ExternalInput")
    b_t = nc.dram_tensor("b_m", [DS, DM], F32, kind="ExternalInput")
    c_t = nc.dram_tensor("c_m", [DM, DS], F32, kind="ExternalInput")
    d_t = nc.dram_tensor("d2", [128, NC_DM], F32, kind="ExternalInput")
    ldt_t = nc.dram_tensor("logdt2", [128, NC_DM], F32, kind="ExternalInput")
    id_t = nc.dram_tensor("ident", [128, 128], F32, kind="ExternalInput")
    out_t = nc.dram_tensor("out", [T, DM], F32, kind="ExternalOutput")

    from contextlib import ExitStack
    with ExitStack() as ctx:
        EC = ctx.enter_context
        BIG = EC(nc.sbuf_tensor("BIG", [128, 16384], F32))
        xnf = EC(nc.sbuf_tensor("xnf", [128, NC_DM * CW], BF16))
        w1r = EC(nc.sbuf_tensor("w1r", [128, RA * 2048], BF16))
        w3r = EC(nc.sbuf_tensor("w3r", [128, RA * 2048], BF16))
        w2r = EC(nc.sbuf_tensor("w2r", [128, RB * 2048], BF16))
        g_sb = EC(nc.sbuf_tensor("g_sb", [128, GF * TH], BF16))
        s_sb = EC(nc.sbuf_tensor("s_sb", [128, 2 * TH], BF16))
        convb = EC(nc.sbuf_tensor("convb", [128, TH], F32))
        kscr = EC(nc.sbuf_tensor("kscr", [128, 1024], F32))
        kt = EC(nc.sbuf_tensor("kt", [128, NC_DM * LL], F32))
        bct = EC(nc.sbuf_tensor("bct", [128, NC_DM * DS], F32))
        pt_sb = EC(nc.sbuf_tensor("pt_sb", [128, NC_DM * DS], F32))
        c_sb = EC(nc.sbuf_tensor("c_sb", [128, NC_DM * DS], F32))
        a_b = EC(nc.sbuf_tensor("a_b", [128, DS], F32))
        dtc = EC(nc.sbuf_tensor("dtc", [128, NC_DM], F32))
        d_sb = EC(nc.sbuf_tensor("d_sb", [128, NC_DM], F32))
        ldt_sb = EC(nc.sbuf_tensor("ldt_sb", [128, NC_DM], F32))
        wg_sb = EC(nc.sbuf_tensor("wg_sb", [128, 64], F32))
        ident = EC(nc.sbuf_tensor("identt", [128, 128], F32))
        diag = EC(nc.sbuf_tensor("diag", [128, 2 * 128], BF16))
        csr = EC(nc.sbuf_tensor("csr", [1, DS], F32))
        csc = EC(nc.sbuf_tensor("csc", [DS, 1], F32))
        ones_c = EC(nc.sbuf_tensor("ones_c", [128, 1], F32))
        ones_r = EC(nc.sbuf_tensor("ones_r", [1, 128], F32))
        stats = EC(nc.sbuf_tensor("stats", [128, 12], F32))
        aggr16 = EC(nc.sbuf_tensor("aggr16", [128, 32], F32))
        rstd16 = EC(nc.sbuf_tensor("rstd16", [128, 16], F32))
        sq16 = EC(nc.sbuf_tensor("sq16", [128, 16], F32))
        lgb = EC(nc.sbuf_tensor("lgb", [128, 8], F32))
        eps_sb = EC(nc.sbuf_tensor("eps_sb", [128, 1], F32))
        rsc = EC(nc.sbuf_tensor("rsc", [128, 48], F32))
        wtok = EC(nc.sbuf_tensor("wtok", [128, 128], F32))
        b_sb = convb
        tmp32 = kscr
        H1 = EC(nc.psum_tensor("H1", [128, 1024], F32))
        H3 = EC(nc.psum_tensor("H3", [128, 1024], F32))
        YA = EC(nc.psum_tensor("YA", [128, 1024], F32))
        YB = EC(nc.psum_tensor("YB", [128, 1024], F32))
        sem_handles = {}
        for k in ("T", "V", "S", "DW", "GFS", "U", "P", "I0", "O"):
            sem_handles[k] = EC(nc.semaphore("sem" + k))

        S = _Sched()

        def vop(fn, waits=()):
            for k, v in waits:
                S.wait("vector", k, v)
            S.wait("vector", "V", S.nv)
            S.emit("vector", lambda E, sems, f=fn: f(E).then_inc(sems["V"], 1))
            S.nv += 1
            return S.nv

        def sop(fn, waits=()):
            for k, v in waits:
                S.wait("scalar", k, v)
            S.wait("scalar", "S", S.ns)
            S.emit("scalar", lambda E, sems, f=fn: f(E).then_inc(sems["S"], 1))
            S.ns += 1
            return S.ns

        def top(fn, waits=(), inc=False):
            for k, v in waits:
                S.wait("tensor", k, v)
            if inc:
                S.emit("tensor", lambda E, sems, f=fn: f(E).then_inc(sems["T"], 1))
                S.nt += 1
                return S.nt
            S.emit("tensor", lambda E, sems, f=fn: f(E))
            return None

        def dx(fn, sem, waits=()):
            for k, v in waits:
                S.wait("sync", k, v)
            S.emit("sync", lambda E, sems, f=fn, sm=sem: f(E).then_inc(sems[sm], 16))

        def dow(fn, waits=()):
            for k, v in waits:
                S.wait("gpsimd", k, v)
            S.emit("gpsimd", lambda E, sems, f=fn: f(E).then_inc(sems["DW"], 16))
            S.ndw += 16

        A = bass.AP

        dx(lambda E: E.dma_start(
            out=A(BIG, 0, [[16384, 128], [1024, 16], [1, 1024]]),
            in_=A(u_t, 0, [[1024, 128], [131072, 16], [1, 1024]])), "U")
        dx(lambda E: E.dma_start(out=ident[:, :], in_=id_t[:, :]), "P")
        dx(lambda E: E.dma_start(
            out=A(c_sb, 0, [[512, 128], [64, 8], [1, 64]]),
            in_=A(c_t, 0, [[64, 128], [8192, 8], [1, 64]])), "P")
        dx(lambda E: E.dma_start(out=b_sb[0:DS, 0:DM], in_=b_t[:, :]), "P")
        dx(lambda E: E.dma_start(out=A(a_b, 0, [[64, 1], [1, 64]]),
                                 in_=a_t[:, :]), "P")
        dx(lambda E: E.dma_start(out=ldt_sb[:, :], in_=ldt_t[:, :]), "P")
        dx(lambda E: E.dma_start(out=d_sb[:, :], in_=d_t[:, :]), "P")
        dx(lambda E: E.dma_start(out=wg_sb[:, :], in_=wg_t[:, :]), "P")
        v_par = 7 * 16

        vop(lambda E: E.memset(ones_c[:, :], 1.0))
        vop(lambda E: E.memset(eps_sb[:, :], 1e-5))
        vop(lambda E: E.memset(ones_r[:, :], 1.0))
        vop(lambda E: E.memset(A(xnf, 0, [[NC_DM * CW, 128], [CW, NC_DM], [1, PADL]]), 0.0))
        v_const = vop(lambda E: E.memset(
            A(xnf, PADL + T, [[NC_DM * CW, 128], [CW, NC_DM], [1, PADR]]), 0.0))

        for cp in range(NC_DM):
            t_csum = top(lambda E, cp=cp: E.matmul(
                YA[0:1, cp * 64:(cp + 1) * 64], ones_c[:, 0:1],
                c_sb[:, cp * 64:(cp + 1) * 64], start=True, stop=True),
                waits=([("P", v_par), ("V", v_const)] if cp == 0 else ()),
                inc=(cp == NC_DM - 1))
        v_csr = vop(lambda E: E.tensor_reduce(
            csr[0:1, 0:DS], A(YA, 0, [[1024, 1], [1, DS], [64, NC_DM]]),
            axis=AX.X, op=OP.add), waits=[("T", t_csum)])
        t_csc = top(lambda E: E.transpose(YB[0:DS, 0:1], csr[0:1, 0:DS], ident[0:1, 0:1]),
                    waits=[("V", v_csr)], inc=True)
        v_csc = vop(lambda E: E.tensor_copy(csc[:, :], YB[0:DS, 0:1]), waits=[("T", t_csc)])
        v_bc = vop(lambda E: E.tensor_scalar(b_sb[0:DS, 0:DM], b_sb[0:DS, 0:DM],
                                             csc[:, 0:1], None, op0=OP.mult))
        for cp in range(NC_DM):
            t_bct = top(lambda E, cp=cp: E.transpose(
                YA[:, cp * 64:(cp + 1) * 64], b_sb[0:DS, cp * 128:(cp + 1) * 128],
                ident[0:DS, 0:DS]),
                waits=([("V", v_bc)] if cp == 0 else ()), inc=(cp == NC_DM - 1))
        v_bct = vop(lambda E: E.tensor_copy(bct[:, :], YA[:, 0:512]), waits=[("T", t_bct)])
        t_ab = top(lambda E: E.matmul(YB[:, 0:DS], ones_r[0:1, 0:128],
                                      A(a_b, 0, [[64, 1], [1, 64]]), start=True, stop=True),
                   waits=[("V", v_csc)], inc=True)
        v_ab = vop(lambda E: E.tensor_copy(a_b[:, :], YB[:, 0:DS]), waits=[("T", t_ab)])
        s_dtc = sop(lambda E: E.activation(dtc[:, :], ldt_sb[:, :], AF.Exp),
                    waits=[("P", v_par)])
        for c in range(NC_DM):
            v_pt = vop(lambda E, c=c: E.tensor_scalar(
                pt_sb[:, c * DS:(c + 1) * DS], a_b[:, :], dtc[:, c:c + 1], None,
                op0=OP.mult), waits=[("S", s_dtc)])
        v_ptdone = v_pt

        for tb in range(16):
            vop(lambda E, tb=tb: E.bn_stats(stats[:, 0:6], BIG[:, tb * 1024:tb * 1024 + 512]),
                waits=[("U", 16)])
            vop(lambda E, tb=tb: E.bn_stats(stats[:, 6:12],
                                            BIG[:, tb * 1024 + 512:tb * 1024 + 1024]))
            vop(lambda E, tb=tb: E.bn_aggr(aggr16[:, 2 * tb:2 * tb + 2], stats[:, 0:12]))
        v_aggr = S.nv
        for tb in range(16):
            sop(lambda E, tb=tb: E.activation(sq16[:, tb:tb + 1],
                                              aggr16[:, 2 * tb + 1:2 * tb + 2],
                                              AF.Sqrt, bias=eps_sb[:, 0:1]),
                waits=[("V", v_aggr)])
        s_sq = S.ns
        for tb in range(16):
            vop(lambda E, tb=tb: E.reciprocal(rstd16[:, tb:tb + 1], sq16[:, tb:tb + 1]),
                waits=[("S", s_sq)])
            vop(lambda E, tb=tb: E.tensor_scalar(
                BIG[:, tb * 1024:(tb + 1) * 1024], BIG[:, tb * 1024:(tb + 1) * 1024],
                aggr16[:, 2 * tb:2 * tb + 1], rstd16[:, tb:tb + 1],
                op0=OP.subtract, op1=OP.mult))
        v_lnapply = S.nv

        ev_half = {}
        ev_clk = {}
        mm_clk = {}
        lg_clk = {}
        for tb in range(16):
            t_half = {}
            for half in range(2):
                for c in range(4 * half, 4 * half + 4):
                    w = [("V", v_lnapply)]
                    if tb > 0:
                        w.append(("V", ev_half[(tb - 1, half)]))
                    elif half == 0 and c == 0:
                        w.append(("V", v_ab))
                    top(lambda E, tb=tb, c=c: E.transpose(
                        YB[:, c * 128:(c + 1) * 128],
                        BIG[:, tb * 1024 + c * 128:tb * 1024 + (c + 1) * 128],
                        ident[:, :]), waits=(w if c == 4 * half else ()),
                        inc=(c == 4 * half + 3))
                t_half[half] = S.nt
            for half in range(2):
                for c in range(4 * half, 4 * half + 4):
                    vop(lambda E, tb=tb, c=c: E.tensor_copy(
                        xnf[:, c * CW + PADL + tb * 128:c * CW + PADL + (tb + 1) * 128],
                        YB[:, c * 128:(c + 1) * 128]), waits=[("T", t_half[half])])
                    ev_clk[(tb, c)] = vop(lambda E, tb=tb, c=c: E.tensor_copy(
                        tmp32[:, c * 128:(c + 1) * 128],
                        YB[:, c * 128:(c + 1) * 128]))
                ev_half[(tb, half)] = ev_clk[(tb, 4 * half + 3)]
            for c in range(NC_DM):
                w = [("V", ev_clk[(tb, c)]), ("P", v_par)]
                if c == 0 and tb > 0:
                    w.append(("V", lg_clk[tb - 1]))
                elif c == 0:
                    w.append(("V", v_bct))
                mm_clk[(tb, c)] = top(lambda E, tb=tb, c=c: E.matmul(
                    YA[:, 0:8], tmp32[:, c * 128:(c + 1) * 128],
                    wg_sb[:, c * 8:(c + 1) * 8], start=(c == 0), stop=(c == NC_DM - 1)),
                    waits=w, inc=True)
            t_logit_last = mm_clk[(tb, NC_DM - 1)]
            m1 = rsc[:, 0:1]; m2 = rsc[:, 1:2]; dd = rsc[:, 2:3]
            w1s = rsc[:, 3:4]; w2s = rsc[:, 4:5]
            l2 = rsc[:, 8:16]; eq1 = rsc[:, 16:24]; eq2 = rsc[:, 24:32]
            lg_clk[tb] = vop(lambda E: E.tensor_copy(lgb[:, :], YA[:, 0:8]),
                             waits=[("T", t_logit_last)])
            vop(lambda E: E.reduce_max(m1, lgb[:, :], axis=AX.X))
            vop(lambda E: E.tensor_scalar(eq1, lgb[:, :], m1, None, op0=OP.is_equal))
            vop(lambda E: E.scalar_tensor_tensor(l2, eq1, -1e30, lgb[:, :],
                                                 op0=OP.mult, op1=OP.add))
            vop(lambda E: E.reduce_max(m2, l2, axis=AX.X))
            vop(lambda E: E.tensor_scalar(eq2, l2, m2, None, op0=OP.is_equal))
            v_d2 = vop(lambda E: E.tensor_tensor(dd, m1, m2, op=OP.subtract))
            s_sig = sop(lambda E: E.activation(w1s, dd, AF.Sigmoid), waits=[("V", v_d2)])
            vop(lambda E: E.tensor_scalar(w2s, w1s, -1.0, 1.0, op0=OP.mult, op1=OP.add),
                waits=[("S", s_sig)])
            vop(lambda E: E.tensor_scalar(eq1, eq1, w1s, None, op0=OP.mult))
            vop(lambda E: E.tensor_scalar(eq2, eq2, w2s, None, op0=OP.mult))
            vop(lambda E, tb=tb: E.tensor_tensor(wtok[:, tb * 8:(tb + 1) * 8],
                                                 eq1, eq2, op=OP.add))
        v_prologue = S.nv
        t_prologue = t_logit_last

        dx(lambda E: E.dma_start(
            out=A(BIG, 0, [[16384, 128], [1024, 16], [1, 1024]]),
            in_=A(u_t, 0, [[1024, 128], [131072, 16], [1, 1024]])), "I0",
           waits=[("T", t_prologue)])

        fg = 0
        yc = 0
        silu_clk = {}
        gmul_clk = {}
        sttv = {}
        tclk_y = {}
        freeA = {}
        freeB = {}
        kred = {}
        extra_h1 = [0]
        extra_h3 = [0]
        extra_y = [0]
        YBUF = [YA, YB]

        def emit_kstep(l):
            half = (l % 2) * 512
            w = [("T", t_prologue)] if l == 0 else []
            if l >= 2:
                w.append(("V", kred[l - 2]))
            s_e = sop(lambda E, l=l, half=half: E.activation(
                kscr[:, half:half + 512], pt_sb[:, :], AF.Exp, scale=float(l)),
                waits=w)
            vop(lambda E, half=half: E.tensor_tensor(
                kscr[:, half:half + 512], kscr[:, half:half + 512], bct[:, :],
                op=OP.mult),
                waits=[("S", s_e), ("V", v_bct), ("V", v_ptdone)])
            kred[l] = vop(lambda E, l=l, half=half: E.tensor_reduce(
                A(kt, l, [[NC_DM * LL, 128], [LL, NC_DM], [1, 1]]),
                A(kscr, half, [[1024, 128], [64, 8], [1, DS]]),
                axis=AX.X, op=OP.add))

        def emit_conv():
            conv_ev = {}
            add_clk = {}
            v_ylast = sttv[yc - 1]
            for c in range(NC_DM):
                dg_clk = {}
                cmm_clk = {}
                for l in range(LL):
                    w = [("V", kred[LL - 1])]
                    if l >= 2:
                        w.append(("T", cmm_clk[l - 2]))
                    dg_clk[l] = vop(lambda E, c=c, l=l: E.tensor_scalar(
                        A(diag, (l % 2) * 128, [[256, 128], [1, 128]]),
                        ident[:, :], kt[:, c * LL + l:c * LL + l + 1], None,
                        op0=OP.mult), waits=w)
                    tw = [("V", dg_clk[l])]
                    if l == 0 and c > 0:
                        tw.append(("V", conv_ev[(c - 1, 1)]))
                    elif l == 0:
                        tw.append(("S", silu_clk[fg - 1]))
                        tw.append(("V", gmul_clk[fg - 1]))
                    first = True
                    for h in range(2):
                        PS = H1 if h == 0 else H3
                        for tc in range(2):
                            top(lambda E, c=c, l=l, tc=tc, h=h, PS=PS: E.matmul(
                                PS[:, tc * 512:(tc + 1) * 512],
                                A(diag, (l % 2) * 128, [[256, 128], [1, 128]]),
                                xnf[:, c * CW + h * TH + tc * 512 + 63 - l:
                                    c * CW + h * TH + tc * 512 + 63 - l + 512],
                                start=(l == 0), stop=(l == LL - 1)),
                                waits=(tw if first else ()),
                                inc=(h == 1 and tc == 1))
                            first = False
                    cmm_clk[l] = S.nt
                for h in range(2):
                    PS = H1 if h == 0 else H3
                    TS = YA if h == 0 else YB
                    conv_ev[(c, h)] = vop(lambda E, c=c, h=h, PS=PS:
                                          E.scalar_tensor_tensor(
                        convb[:, :],
                        xnf[:, c * CW + PADL + h * TH:c * CW + PADL + (h + 1) * TH],
                        d_sb[:, c:c + 1], PS[:, :], op0=OP.mult, op1=OP.add),
                        waits=[("T", cmm_clk[LL - 1])])
                    tw = [("V", conv_ev[(c, h)])]
                    if c > 0 or h > 0:
                        prev = (c, 0) if h == 1 else (c - 1, 1)
                        tw.append(("V", add_clk[(prev, 7)]))
                    else:
                        tw.append(("V", v_ylast))
                    for tb in range(8):
                        top(lambda E, tb=tb, TS=TS: E.transpose(
                            TS[:, tb * 128:(tb + 1) * 128],
                            convb[:, tb * 128:(tb + 1) * 128], ident[:, :]),
                            waits=(tw if tb == 0 else ()), inc=(tb == 7))
                    t_ctr = S.nt
                    add_clk[((c, h), 7)] = vop(
                        lambda E, c=c, h=h, TS=TS: E.tensor_tensor(
                            A(BIG, h * 8192 + c * 128,
                              [[16384, 128], [1024, 8], [1, 128]]),
                            A(BIG, h * 8192 + c * 128,
                              [[16384, 128], [1024, 8], [1, 128]]),
                            TS[:, :], op=OP.add),
                        waits=[("T", t_ctr), ("I0", 16)])
            extra_h1[0] = conv_ev[(NC_DM - 1, 0)]
            extra_h3[0] = conv_ev[(NC_DM - 1, 1)]
            extra_y[0] = add_clk[((NC_DM - 1, 1), 7)]

        for e in range(NE):
            for g in range(NG):
                segi = e * NG + g
                for j in range(4):
                    gsA = segi * 4 + j
                    pA = gsA % RA
                    pB = gsA % RB
                    fcol = g * 1024 + j * 256
                    if gsA >= RA:
                        S.wait("gpsimd", "T", freeA[gsA - RA])
                    dow(lambda E, e=e, fcol=fcol, pA=pA: E.dma_start(
                        out=A(w1r, pA * 2048, [[RA * 2048, 128], [256, 8], [1, 256]]),
                        in_=A(w1_t, e * DM * FF + fcol,
                              [[FF, 128], [128 * FF, 8], [1, 256]])))
                    dow(lambda E, e=e, fcol=fcol, pA=pA: E.dma_start(
                        out=A(w3r, pA * 2048, [[RA * 2048, 128], [256, 8], [1, 256]]),
                        in_=A(w3_t, e * DM * FF + fcol,
                              [[FF, 128], [128 * FF, 8], [1, 256]])))
                    if gsA >= RB:
                        S.wait("gpsimd", "T", freeB[gsA - RB])
                    dow(lambda E, e=e, fcol=fcol, pB=pB: E.dma_start(
                        out=A(w2r, pB * 2048, [[RB * 2048, 128], [1024, 2], [1, 1024]]),
                        in_=A(w2_t, e * FF * DM + fcol * DM,
                              [[DM, 128], [128 * DM, 2], [1, 1024]])))
                    S.wait("gpsimd", "DW", S.ndw)
                    S.emit("gpsimd", lambda E, sems: E.sem_inc(sems["GFS"], 1))
                    S.ngf += 1
                for h in range(2):
                    for fl in range(GF):
                        gsA = segi * 4 + fl // 2
                        pA = gsA % RA
                        scol = pA * 2048 + (fl % 2) * 128
                        w = [("GFS", segi * 4 + fl // 2 + 1), ("V", v_prologue)]
                        if fg > 0:
                            w.append(("S", silu_clk[fg - 1]))
                        if extra_h1[0]:
                            w.append(("V", extra_h1[0])); extra_h1[0] = 0
                        for k in range(NC_DM):
                            for tc in range(2):
                                top(lambda E, k=k, tc=tc, scol=scol, h=h: E.matmul(
                                    H1[:, tc * 512:(tc + 1) * 512],
                                    w1r[:, scol + k * 256:scol + k * 256 + 128],
                                    xnf[:, k * CW + PADL + h * TH + tc * 512:
                                        k * CW + PADL + h * TH + tc * 512 + 512],
                                    start=(k == 0), stop=(k == NC_DM - 1)),
                                    waits=(w if (k == 0 and tc == 0) else ()),
                                    inc=(k == NC_DM - 1 and tc == 1))
                        t_h1 = S.nt
                        w = []
                        if fg > 0:
                            w.append(("V", gmul_clk[fg - 1]))
                        if extra_h3[0]:
                            w.append(("V", extra_h3[0])); extra_h3[0] = 0
                        for k in range(NC_DM):
                            for tc in range(2):
                                top(lambda E, k=k, tc=tc, scol=scol, h=h: E.matmul(
                                    H3[:, tc * 512:(tc + 1) * 512],
                                    w3r[:, scol + k * 256:scol + k * 256 + 128],
                                    xnf[:, k * CW + PADL + h * TH + tc * 512:
                                        k * CW + PADL + h * TH + tc * 512 + 512],
                                    start=(k == 0), stop=(k == NC_DM - 1)),
                                    waits=(w if (k == 0 and tc == 0) else ()),
                                    inc=(k == NC_DM - 1 and tc == 1))
                        t_h3 = S.nt
                        if h == 1 and fl % 2 == 1:
                            freeA[gsA] = t_h3
                        w = [("T", t_h1)]
                        if fg >= 2:
                            w.append(("V", gmul_clk[fg - 2]))
                        silu_clk[fg] = sop(lambda E, fl=fl: E.activation(
                            s_sb[:, (fl % 2) * TH:(fl % 2 + 1) * TH],
                            H1[:, :], AF.Silu), waits=w)
                        w = [("T", t_h3), ("S", silu_clk[fg])]
                        if yc > 0:
                            w.append(("T", tclk_y.get(yc - 1, 0) or 0))
                        gmul_clk[fg] = vop(lambda E, fl=fl: E.tensor_tensor(
                            g_sb[:, fl * TH:(fl + 1) * TH],
                            s_sb[:, (fl % 2) * TH:(fl % 2 + 1) * TH],
                            H3[:, :], op=OP.mult), waits=w)
                        if e < 2 and fg < LL:
                            emit_kstep(fg)
                        fg += 1
                    for tb in range(8):
                        buf = YBUF[yc % 2]
                        yt = h * 8 + tb
                        w = [("V", gmul_clk[fg - 1]), ("GFS", segi * 4 + 4)]
                        if yc >= 2:
                            w.append(("V", sttv[yc - 2]))
                        if extra_y[0]:
                            w.append(("V", extra_y[0])); extra_y[0] = 0
                        for fl in range(GF):
                            pB = (segi * 4 + fl // 2) % RB
                            for dmh in range(2):
                                top(lambda E, fl=fl, dmh=dmh, tb=tb, pB=pB, buf=buf:
                                    E.matmul(
                                        buf[:, dmh * 512:(dmh + 1) * 512],
                                        g_sb[:, fl * TH + tb * 128:fl * TH + (tb + 1) * 128],
                                        w2r[:, pB * 2048 + (fl % 2) * 1024 + dmh * 512:
                                            pB * 2048 + (fl % 2) * 1024 + dmh * 512 + 512],
                                        start=(fl == 0), stop=(fl == GF - 1)),
                                    waits=(w if (fl == 0 and dmh == 0) else ()),
                                    inc=(fl == GF - 1 and dmh == 1))
                        tclk_y[yc] = S.nt
                        sttv[yc] = vop(lambda E, yt=yt, e=e, buf=buf:
                                       E.scalar_tensor_tensor(
                                           BIG[:, yt * 1024:(yt + 1) * 1024],
                                           buf[:, :],
                                           wtok[:, yt * 8 + e:yt * 8 + e + 1],
                                           BIG[:, yt * 1024:(yt + 1) * 1024],
                                           op0=OP.mult, op1=OP.add),
                                       waits=[("T", tclk_y[yc]), ("I0", 16)])
                        yc += 1
                for j in range(4):
                    freeB[segi * 4 + j] = tclk_y[yc - 1]
            if e == 1:
                emit_conv()
        dx(lambda E: E.dma_start(
            out=A(out_t, 0, [[1024, 128], [131072, 16], [1, 1024]]),
            in_=A(BIG, 0, [[16384, 128], [1024, 16], [1, 1024]])), "O",
           waits=[("V", sttv[yc - 1])])
        S.wait("sync", "O", 16)

        with nc.Block() as block:
            @block.sync
            def _(E):
                for fn in S.ops["sync"]:
                    fn(E, sem_handles)

            @block.gpsimd
            def _(E):
                for fn in S.ops["gpsimd"]:
                    fn(E, sem_handles)

            @block.scalar
            def _(E):
                for fn in S.ops["scalar"]:
                    fn(E, sem_handles)

            @block.vector
            def _(E):
                for fn in S.ops["vector"]:
                    fn(E, sem_handles)

            @block.tensor
            def _(E):
                for fn in S.ops["tensor"]:
                    fn(E, sem_handles)
    return nc


def _layouts(inputs):
    Wg = np.ascontiguousarray(np.asarray(inputs["L0_ffn_Wg"], np.float32))
    wg2 = np.ascontiguousarray(Wg.reshape(8, 128, 8).transpose(1, 0, 2).reshape(128, 64))
    d2 = np.ascontiguousarray(np.asarray(inputs["L0_attn_D"], np.float32).reshape(8, 128).T)
    ldt2 = np.ascontiguousarray(
        np.asarray(inputs["L0_attn_log_dt"], np.float32).reshape(8, 128).T)
    ident = np.eye(128, dtype=np.float32)
    return wg2, d2, ldt2, ident


def kernel(L0_attn_u, L0_attn_A, L0_attn_B, L0_attn_C, L0_attn_D, L0_attn_log_dt,
           L0_ffn_Wg, L0_ffn_W1, L0_ffn_W3, L0_ffn_W2):
    u = np.ascontiguousarray(np.asarray(L0_attn_u, np.float32))
    W1 = np.ascontiguousarray(np.asarray(L0_ffn_W1, np.float32))
    W3 = np.ascontiguousarray(np.asarray(L0_ffn_W3, np.float32))
    W2 = np.ascontiguousarray(np.asarray(L0_ffn_W2, np.float32))
    Am = np.ascontiguousarray(np.asarray(L0_attn_A, np.float32).reshape(1, DS))
    Bm = np.ascontiguousarray(np.asarray(L0_attn_B, np.float32))
    Cm = np.ascontiguousarray(np.asarray(L0_attn_C, np.float32))
    ins = {"L0_ffn_Wg": L0_ffn_Wg, "L0_attn_D": L0_attn_D,
           "L0_attn_log_dt": L0_attn_log_dt}
    wg2, d2, ldt2, ident = _layouts(ins)
    nc = build_nc()
    in_maps = []
    for b in range(NCORES):
        in_maps.append({
            "u": np.ascontiguousarray(u[b]),
            "w1": W1, "w3": W3, "w2": W2, "wg2": wg2,
            "a_v": Am, "b_m": Bm, "c_m": Cm, "d2": d2, "logdt2": ldt2,
            "ident": ident,
        })
    res = run_bass_kernel_spmd(nc, in_maps, core_ids=list(range(NCORES)))
    out = np.stack([np.asarray(res.results[b]["out"]) for b in range(NCORES)], axis=0)
    return out


if __name__ == "__main__":
    nc = build_nc()
    print("build ok")
